# version 12 (final)
# speedup vs baseline: 1.2525x; 1.0124x over previous
import numpy as np
import ml_dtypes
import concourse.bass as bass
import concourse.mybir as mybir
from concourse.bass import IndirectOffsetOnAxis
from concourse.bass_utils import run_bass_kernel_spmd

F32 = mybir.dt.float32
BF16 = mybir.dt.bfloat16
U32 = mybir.dt.uint32
AF = mybir.ActivationFunctionType
ALU = mybir.AluOpType
AXX = mybir.AxisListType.X

T, D, TT, DC = 2048, 1024, 16, 8
H, DH, M = 16, 64, 256
DI, NS, NE, K = 2048, 16, 16, 256
DIT = 16
LNEPS = 1e-5
PI = 3.141592653589793
MAGIC = 12582912.0
INV2PI = 1.0 / (2 * PI)
NBIS = 27

ENGS = ("te", "ve", "sc", "gp", "sy")
DMAOPS = ("dma_start", "dma_start_transpose", "indirect_dma_start")

_ANY = ("*", 0, 10 ** 9, 0, 10 ** 9)


def _apspan(ap):
    try:
        tn = ap.tensor.name
        rowlen = int(ap.tensor.shape[-1])
        bp = int(ap.base_partition())
        shp = list(ap.shape)
        npart = int(shp[0]) if shp else 1
        c0 = int(ap.offset) - bp * rowlen
        if len(shp) == 2:
            st = max(int(ap.stride()), 0)
            span = (shp[1] - 1) * st + 1 if shp[1] > 0 else 1
        elif len(shp) == 1:
            span = 1
        else:
            c0, span = 0, rowlen
        return (tn, bp, bp + npart, c0, c0 + span)
    except Exception:
        return _ANY


def _spans_hit(a, b):
    if a[0] == "*" or b[0] == "*":
        return True
    return (a[0] == b[0] and a[1] < b[2] and b[1] < a[2]
            and a[3] < b[4] and b[3] < a[4])


class Rec:
    def __init__(self):
        self.calls, self.ndma = [], 0

    def __getattr__(self, name):
        if name.startswith("_"):
            raise AttributeError(name)
        def call(*a, **kw):
            self.calls.append((name, a, kw))
            if name in DMAOPS:
                self.ndma += 1
        return call


class P:
    def __init__(self, nc):
        self.nc = nc
        self.semd = {e: nc.alloc_semaphore("semd_" + e) for e in ENGS}
        self.semc = {e: nc.alloc_semaphore("semc_" + e) for e in ENGS}
        self.semh = {e: nc.alloc_semaphore("semh_" + e) for e in ("ve", "sc", "gp")}
        self.cnt = {(e, k): 0 for e in ENGS for k in "dc"}
        self.steps = {e: [] for e in ENGS}

    ENGAPI = None
    dma_counts = None
    measured = None

    def run(self, eng, wait=(), build=None, defer=False):
        rec = Rec()
        build(rec)
        rec._wait = [w for w in wait if w is not None]
        rec._defer = defer
        idx = len(self.steps[eng])
        self.steps[eng].append(rec)
        if rec.ndma:
            assert all(n in DMAOPS for (n, a, kw) in rec.calls), "no mixed dma/compute steps"
            if self.dma_counts is not None:
                inc = self.dma_counts[(eng, idx)]
            else:
                inc = 16 * rec.ndma
            self.cnt[(eng, "d")] += inc
            rec._dtarget = self.cnt[(eng, "d")]
        self.cnt[(eng, "c")] += 1
        ev = (eng, "c", self.cnt[(eng, "c")])
        if defer:
            return ev, (eng, "d", rec._dtarget)
        return ev

    def emit(self, block, measure=False):
        if measure:
            self.measured = {}
        def body(eng_name):
            def _(api):
                track = eng_name in ("ve", "sc", "gp")
                log = []
                floor = 0
                nh = 0
                nstep = 0
                wm_h, wm_c = 0, 0

                def adv():
                    nonlocal floor
                    while floor < len(log):
                        ck, cv = log[floor][2], log[floor][3]
                        if (ck == "h" and cv <= wm_h) or (ck == "c" and cv <= wm_c):
                            floor += 1
                        else:
                            break

                for idx, rec in enumerate(self.steps[eng_name]):
                    nstep = idx + 1
                    if not measure:
                        for (we, wk, wv) in rec._wait:
                            api.wait_ge(self.semc[we] if wk == "c" else self.semd[we], wv)
                            if wk == "c" and we == eng_name:
                                wm_c = max(wm_c, wv)
                        adv()
                    nlast = len(rec.calls) - 1
                    n0 = self.nc.n_instructions()
                    for i, (name, a, kw) in enumerate(rec.calls):
                        iscomp = rec.ndma == 0
                        if track and iscomp and name == "drain":
                            r = api.drain()
                            wm_h, wm_c = nh, nstep - 1
                            adv()
                            if i == nlast:
                                r.then_inc(self.semc[eng_name], 1)
                                log.append(([], [], "c", nstep))
                            continue
                        writes, reads = [], []
                        if track and iscomp:
                            if a and hasattr(a[0], "base_partition"):
                                writes.append(_apspan(a[0]))
                            for k in ("out", "ap", "accum_out"):
                                if k in kw and hasattr(kw[k], "base_partition"):
                                    writes.append(_apspan(kw[k]))
                            for o in list(a[1:]) + [v for k, v in kw.items()
                                                    if k not in ("out", "ap", "accum_out")]:
                                if hasattr(o, "base_partition"):
                                    reads.append(_apspan(o))
                            vh = vc = 0
                            for j in range(len(log) - 1, floor - 1, -1):
                                pw, pr, ck, cv = log[j]
                                if ck == "h" and (cv <= wm_h or cv <= vh):
                                    continue
                                if ck == "c" and (cv <= wm_c or cv <= vc):
                                    continue
                                hit = any(_spans_hit(rd, w) for rd in reads for w in pw) or \
                                      any(_spans_hit(wr, o) for wr in writes for o in pw + pr)
                                if hit:
                                    if ck == "h":
                                        vh = max(vh, cv)
                                    else:
                                        vc = max(vc, cv)
                            if vh > wm_h:
                                api.wait_ge(self.semh[eng_name], vh)
                                wm_h = vh
                            if vc > wm_c:
                                api.wait_ge(self.semc[eng_name], vc)
                                wm_c = vc
                            if vh or vc:
                                adv()
                        r = getattr(api, name)(*a, **kw)
                        if name in DMAOPS:
                            r.then_inc(self.semd[eng_name], 16)
                        elif iscomp and i == nlast:
                            r.then_inc(self.semc[eng_name], 1)
                            if track:
                                log.append((writes, reads, "c", nstep))
                        elif track and iscomp:
                            nh += 1
                            r.then_inc(self.semh[eng_name], 1)
                            log.append((writes, reads, "h", nh))
                    if measure and rec.ndma:
                        self.measured[(eng_name, idx)] = 16 * (self.nc.n_instructions() - n0)
                    if rec.ndma:
                        if not rec._defer:
                            tgt = 0 if measure else rec._dtarget
                            api.wait_ge(self.semd[eng_name], tgt)
                        api.nop().then_inc(self.semc[eng_name], 1)
            return _
        for e, meth in (("sy", block.sync), ("gp", block.gpsimd), ("te", block.tensor),
                        ("ve", block.vector), ("sc", block.scalar)):
            if self.steps[e]:
                meth(body(e))

    def bar(self):
        return [(e, "c", self.cnt[(e, "c")]) for e in ENGS if self.cnt[(e, "c")]]


def ts(i, n):
    return slice(i * n, (i + 1) * n)


def build(upto=4, _counts=None):
    nc = bass.Bass("TRN2", target_bir_lowering=False, debug=False)
    p = P(nc)
    p.dma_counts = _counts
    dd = {}

    def din(name, shape, dt=F32):
        dd[name] = nc.dram_tensor(name, list(shape), dt, kind="ExternalInput").ap()
        return dd[name]

    def dint(name, shape, dt):
        dd[name] = nc.dram_tensor(name, list(shape), dt, kind="Internal").ap()
        return dd[name]

    S = lambda name, shape, dt: nc.alloc_sbuf_tensor(name, list(shape), dt)

    x_in = din("x", [T, D])
    wqT = din("wqT", [D, D]); wkT = din("wkT", [D, D]); wvT = din("wvT", [D, D]); woT = din("woT", [D, D])
    rfTh = din("rfTh", [128, M])
    in_projW = din("in_projW", [D, 2 * DI])
    conv_wi = din("conv_wi", [DI, 4]); conv_bi = din("conv_bi", [DI, 1])
    x_projW = din("x_projW", [DI, 80])
    A_logi = din("A_logi", [DI, NS])
    dt_wi = din("dt_wi", [DI, 1]); dt_bi = din("dt_bi", [DI, 1])
    out_projW = din("out_projW", [DI, D])
    ln_gi = din("ln_gi", [1, D]); ln_bi = din("ln_bi", [1, D])
    WgW = din("WgW", [D, NE]); bgi = din("bgi", [NE, 1])
    w1i = din("w1i", [NE, D, D]); w2i = din("w2i", [NE, D, D]); w3i = din("w3i", [NE, D, D])
    bei = din("bei", [1, 3 * NE])
    onesr = din("onesr", [1, 128])
    kkrep = din("kkrep", [128, 256])
    bm2 = din("bm2", [128, 2])
    selbi = din("selbi", [16, 16 * 128])
    idfi = din("idfi", [128, 128])
    y_out = nc.dram_tensor("y", [T, D], F32, kind="ExternalOutput").ap()

    xnT_d = dint("xnT_d", [T, 1040], BF16)
    vT_d = dint("vT_d", [H, T, DH], BF16)
    aoP_d = dint("aoP_d", [8, T, 128], BF16)
    gat_d = dint("gat_d", [NE * K, 1040], BF16)
    h3F_d = dint("h3F_d", [D, NE * K], BF16)
    acc_d = dint("acc_d", [T, D], F32)
    probT_d = dint("probT_d", [T, NE], F32)
    zf_d = dint("zf_d", [DI, T], BF16)
    yzF_d = dint("yzF_d", [DI, T], BF16)
    BCD_d = dint("BCD_d", [2 * NS + 1, 128, T], F32)

    xres = S("xres", [128, TT * D], F32)
    onesc = S("onesc", [1, 128], F32)
    kk = S("kk", [128, 256], F32)
    bmask2 = S("bmask2", [128, 2], F32)
    Acol = S("Acol", [128, DIT * NS], F32)
    dtwc = S("dtwc", [128, DIT], F32); dtbc = S("dtbc", [128, DIT], F32)
    cwc = S("cwc", [128, DIT * 4], F32); cbc = S("cbc", [128, DIT], F32)
    bgc = S("bgc", [16, 1], F32)
    bexp = S("bexp", [128, 3 * NE], F32)
    lnst = S("lnst", [128, 8], F32)
    rfsb = S("rfsb", [128, M], F32)
    cols = S("cols", [128, 8], F32)

    sckv = S("sckv", [128, H * TT], F32)
    scq = S("scq", [128, H * TT], F32)
    thrc = S("thrc", [16, 8], F32)
    gam = S("gam", [128, D], F32)
    bet = S("bet", [128, D], F32)
    onef = S("onef", [16, 512], F32)
    posh = S("posh", [16, T], mybir.dt.float16)
    onesb = S("onesb", [128, 1], BF16)
    onesrb = S("onesrb", [128, 128], BF16)
    selb = S("selb", [16, 16 * 128], BF16)
    idf = S("idf", [128, 128], F32)
    kkh = S("kkh", [128, 256], mybir.dt.float16)
    posT = S("posT", [128, TT * NE], mybir.dt.float16)
    valsb = S("valsb", [128, 2 * NE], F32)
    cnt2 = S("cnt2", [16, 4], F32)
    idxc = S("idxc", [128, 32], U32)
    idxf = S("idxf", [128, 32], F32)
    KVs = S("KVs", [128, H * 4 * 66], BF16)

    A0b = S("A0b", [128, 8192], BF16)
    A1b = S("A1b", [128, 16384], BF16)
    A4b = S("A4b", [128, 4352], BF16)
    A5f = S("A5f", [128, 4096], F32)
    A6f = S("A6f", [128, 4096], F32)
    A7f = S("A7f", [128, 2048], F32)
    A8f = S("A8f", [128, 2048], F32)
    A9f = A5f[:, 0:1024]
    lg = A5f[0:16, 0:2048]
    msk = A5f[0:16, 2048:4096]
    pos = A6f[0:16, 0:2048]

    psb = [nc.alloc_psum_tensor(f"ps{i}", [128, 512], F32) for i in range(8)]
    g_row = A6f[0:1, 0:1024]
    b_row = A6f[0:1, 1024:2048]
    be_row = A6f[0:1, 2048:2048 + 3 * NE]


    r = Rec()
    def boot_loads(r):
        r.dma_start(out=xres[:, :].rearrange("p (tt d) -> p tt d", tt=TT), in_=x_in.rearrange("(tt p) d -> p tt d", p=128))
        r.dma_start(out=onesc[:, :], in_=onesr[:, :])
        r.dma_start(out=kk[:, :], in_=kkrep[:, :])
        r.dma_start(out=bmask2[:, :], in_=bm2[:, :])
        r.dma_start(out=bgc[:, :], in_=bgi[:, :])
        r.dma_start(out=dtwc[:, :].rearrange("p (tt o) -> p tt o", o=1), in_=dt_wi.rearrange("(tt p) o -> p tt o", p=128), allow_slow_non_contiguous=True)
        r.dma_start(out=dtbc[:, :].rearrange("p (tt o) -> p tt o", o=1), in_=dt_bi.rearrange("(tt p) o -> p tt o", p=128), allow_slow_non_contiguous=True)
        r.dma_start(out=cwc[:, :].rearrange("p (tt k) -> p tt k", k=4), in_=conv_wi.rearrange("(tt p) k -> p tt k", p=128))
        r.dma_start(out=cbc[:, :].rearrange("p (tt o) -> p tt o", o=1), in_=conv_bi.rearrange("(tt p) o -> p tt o", p=128), allow_slow_non_contiguous=True)
        r.dma_start(out=Acol[:, :].rearrange("p (tt s) -> p tt s", s=NS), in_=A_logi.rearrange("(tt p) s -> p tt s", p=128))
        r.dma_start(out=g_row[:, :], in_=ln_gi[:, :])
        r.dma_start(out=b_row[:, :], in_=ln_bi[:, :])
        r.dma_start(out=be_row[:, :], in_=bei[:, :])
    ev_ld = p.run("sy", build=boot_loads)
    ev_rf = p.run("gp", build=lambda r: (r.dma_start(out=rfsb[:, :], in_=rfTh[:, :]),
                                          r.dma_start(out=selb[:, :], in_=selbi[:, :]),
                                          r.dma_start(out=idf[:, :], in_=idfi[:, :])))
    ev_A = p.run("sc", wait=[ev_ld], build=lambda r: r.activation(
        out=Acol[:, :], in_=Acol[:, :], func=AF.Exp, bias=0.0, scale=1.0))
    ev_A2 = p.run("ve", wait=[ev_A], build=lambda r: r.tensor_scalar(
        out=Acol[:, :], in0=Acol[:, :], scalar1=-1.0, scalar2=None, op0=ALU.mult))
    def mkcols(r):
        r.memset(cols[:, 0:1], PI / 2)
        r.memset(cols[:, 1:2], -PI)
        r.memset(cols[:, 2:3], float(np.log(1.0 / 16.0)))
        r.memset(cols[:, 3:4], LNEPS)
        r.memset(cols[:, 4:5], 1.0)
        r.memset(onef[:, :], 1.0)
        r.memset(onesb[:, :], 1.0)
        r.memset(onesrb[:, :], 1.0)
    ev_cols = p.run("ve", build=mkcols)
    def bcast(r):
        r.matmul(psb[0][:, 0:512], onesc[:, :], g_row[:, 0:512], start=True, stop=True)
        r.matmul(psb[1][:, 0:512], onesc[:, :], g_row[:, 512:1024], start=True, stop=True)
        r.matmul(psb[2][:, 0:512], onesc[:, :], b_row[:, 0:512], start=True, stop=True)
        r.matmul(psb[3][:, 0:512], onesc[:, :], b_row[:, 512:1024], start=True, stop=True)
        r.matmul(psb[4][:, 0:3 * NE], onesc[:, :], be_row[:, :], start=True, stop=True)
    import os as _os
    if _os.environ.get("K1NOBCAST"):
        bcast = lambda r: r.nop()
        bcev = lambda r: r.nop()
    ev_bc = p.run("te", wait=[ev_ld], build=bcast)
    def bcev(r):
        r.tensor_copy(gam[:, 0:512], psb[0][:, 0:512])
        r.tensor_copy(gam[:, 512:1024], psb[1][:, 0:512])
        r.tensor_copy(bet[:, 0:512], psb[2][:, 0:512])
        r.tensor_copy(bet[:, 512:1024], psb[3][:, 0:512])
        r.tensor_copy(bexp[:, :], psb[4][:, 0:3 * NE])
    ev_bc2 = p.run("ve", wait=[ev_bc], build=bcev)
    boot = [ev_ld, ev_rf, ev_A2, ev_cols, ev_bc2]

    def apply_ln(wait_evs, affine):
        out_evs, half_evs, prev = [], [], None
        for tt in range(TT):
            xt = xres[:, ts(tt, D)]
            e1 = p.run("ve", wait=wait_evs, build=lambda r, xt=xt: r.reduce_sum(
                lnst[:, 0:1], xt, axis=AXX))
            e2 = p.run("sc", wait=wait_evs + [prev], build=lambda r, xt=xt: r.activation(
                out=A9f, in_=xt, func=AF.Square, bias=0.0, scale=1.0))
            e3 = p.run("ve", wait=[e2], build=lambda r: r.reduce_sum(
                lnst[:, 1:2], A9f, axis=AXX))
            def stats(r):
                r.mul(lnst[:, 2:3], lnst[:, 0:1], 1.0 / D)
                r.drain()
                r.activation(out=lnst[:, 3:4], in_=lnst[:, 2:3], func=AF.Square,
                             bias=0.0, scale=1.0)
            e4 = p.run("sc", wait=[e1, e3], build=stats)
            e5 = p.run("ve", wait=[e4], build=lambda r: r.scalar_tensor_tensor(
                out=lnst[:, 4:5], in0=lnst[:, 1:2], scalar=1.0 / D,
                in1=lnst[:, 3:4], op0=ALU.mult, op1=ALU.subtract))
            def invf(r):
                r.activation(out=lnst[:, 5:6], in_=lnst[:, 4:5], func=AF.Ln,
                             bias=cols[:, 3:4], scale=1.0)
                r.drain()
                r.activation(out=lnst[:, 6:7], in_=lnst[:, 5:6], func=AF.Exp,
                             bias=0.0, scale=-0.5)
            e6 = p.run("sc", wait=[e5], build=invf)
            prev = e6
            dst = A0b[:, ts(tt % 8, D)]
            if not affine:
                e7 = p.run("ve", wait=[e6, half_evs[-1] if tt >= 8 and half_evs else None],
                           build=lambda r, xt=xt, dst=dst: r.tensor_scalar(
                    out=dst, in0=xt, scalar1=lnst[:, 2:3],
                    scalar2=lnst[:, 6:7], op0=ALU.subtract, op1=ALU.mult))
            else:
                def aff(r, xt=xt, dst=dst):
                    r.tensor_scalar(out=A9f, in0=xt, scalar1=lnst[:, 2:3],
                                    scalar2=lnst[:, 6:7], op0=ALU.subtract, op1=ALU.mult)
                    r.tensor_mul(A9f, A9f, gam[:, :])
                    r.tensor_tensor(out=dst, in0=A9f, in1=bet[:, :], op=ALU.add)
                e7 = p.run("ve", wait=[e6, half_evs[-1] if tt >= 8 and half_evs else None],
                           build=aff)
            out_evs.append(e7)
            if tt % 8 == 7:
                hw = p.run("sy", wait=out_evs[-8:], build=lambda r, tt=tt: r.dma_start(
                    out=xnT_d[(tt - 7) * 128:(tt + 1) * 128, 0:D].rearrange(
                        "(t8 p) d -> p t8 d", p=128),
                    in_=A0b[:, :].rearrange("p (t8 d) -> p t8 d", t8=8)))
                half_evs.append(hw)
        return half_evs[-1], half_evs

    def read_xnF(ev_ready, dst):
        evs = []
        for dc2 in range(DC):
            evs.append(p.run("sy", wait=[ev_ready] + (evs[-1:] if evs else []),
                build=lambda r, dc2=dc2:
                r.dma_start_transpose(out=dst[:, ts(dc2, T)], in_=xnT_d[:, ts(dc2, 128)])))
        return evs

    STAGE = 9
    def attn_a(ev_prev):
        if STAGE <= 0:
            return ev_prev
        ev_xnw, lnhalves = apply_ln([ev_prev] + boot, affine=False)
        if STAGE <= 1:
            return ev_xnw
        ev_xnf = read_xnF(lnhalves[1], A1b)
        if STAGE <= 2:
            return ev_xnf[-1]

        ev_v = []
        evwv = None
        for ec in range(DC):
            evwv = p.run("gp", wait=[ev_xnw] + (ev_v[-1:] if ev_v else []), build=lambda r, ec=ec: r.dma_start(
                out=A4b[:, 0:1024].rearrange("p (dc e) -> p dc e", dc=DC),
                in_=wvT[:, ts(ec, 128)].rearrange("(dc p) e -> p dc e", p=128)))
            sub = []
            for tg in range(2):
                def mmv(r, tg=tg, ec=ec):
                    for j in range(8):
                        tt = 8 * tg + j
                        for dc2 in range(DC):
                            r.matmul(psb[j][:, 0:128],
                                     A1b[:, dc2 * T + tt * 128: dc2 * T + (tt + 1) * 128],
                                     A4b[:, dc2 * 128: (dc2 + 1) * 128],
                                     start=(dc2 == 0), stop=(dc2 == DC - 1))
                evm = p.run("te", wait=[evwv] + (sub[-1:] if sub else (ev_v[-1:] if ev_v else [])) + ev_xnf,
                            build=mmv)
                def evv(r, tg=tg):
                    for j in range(8):
                        tt = 8 * tg + j
                        r.tensor_copy(A0b[:, tt * 128: (tt + 1) * 128], psb[j][:, 0:128])
                sub.append(p.run("ve", wait=[evm], build=evv))
            def wv(r, ec=ec):
                for j in range(2):
                    r.dma_start(
                        out=vT_d[2 * ec + j].rearrange("(tt p) e -> p tt e", p=128),
                        in_=A0b[:, 0:2048].rearrange("p (tt ee) -> p tt ee", tt=TT)[:, :, ts(j, DH)])
            ev_v.append(p.run("sy", wait=sub, build=wv))
        ev_vall = ev_v[-1]
        if STAGE <= 3:
            return ev_vall


        KFb, QFb = 0, 4096
        ev_scl = None
        ev_h_last = None
        ev_rhs_prev = None
        ev_epi_prev = None
        ev_ao = []
        for hp in range(8):
            evwq = p.run("gp", wait=[ev_vall, ev_h_last], build=lambda r, hp=hp: r.dma_start(
                out=A4b[:, 0:1024].rearrange("p (dc e) -> p dc e", dc=DC),
                in_=wqT[:, ts(hp, 128)].rearrange("(dc p) e -> p dc e", p=128)))
            evwk = p.run("gp", wait=[ev_vall, ev_h_last], build=lambda r, hp=hp: r.dma_start(
                out=A4b[:, 1024:2048].rearrange("p (dc e) -> p dc e", dc=DC),
                in_=wkT[:, ts(hp, 128)].rearrange("(dc p) e -> p dc e", p=128)))
            dep = [evwq, evwk, ev_vall, ev_h_last]
            wqbuf = A4b[:, 0:1024]
            wkbuf = A4b[:, 1024:2048]
            ev_pq = []
            for tg in range(2):
                def mmqp(r, tg=tg, hp=hp, wqbuf=wqbuf):
                    for tc2 in range(2):
                        tc = 2 * tg + tc2
                        for dc2 in range(DC):
                            r.matmul(psb[tc2][:, 0:512],
                                     wqbuf[:, ts(dc2, 128)],
                                     A1b[:, dc2 * T + tc * 512: dc2 * T + tc * 512 + 512],
                                     start=(dc2 == 0), stop=(dc2 == DC - 1))
                        for dc2 in range(DC):
                            r.matmul(psb[2 + tc2][:, 0:512],
                                     wkbuf[:, ts(dc2, 128)],
                                     A1b[:, dc2 * T + tc * 512: dc2 * T + tc * 512 + 512],
                                     start=(dc2 == 0), stop=(dc2 == DC - 1))
                evm = p.run("te", wait=dep + (ev_pq[-1:] if ev_pq else []), build=mmqp)
                def evqp(r, tg=tg):
                    for tc2 in range(2):
                        tc = 2 * tg + tc2
                        r.tensor_copy(A5f[:, ts(tc, 512)], psb[tc2][:, 0:512])
                        r.tensor_copy(A6f[:, ts(tc, 512)], psb[2 + tc2][:, 0:512])
                ev_pq.append(p.run("ve", wait=[evm], build=evqp))
            ev_qp = ev_pq[-1]
            def sqq(r):
                r.activation(out=A7f[:, :], in_=A5f[:, 0:2048], func=AF.Square, bias=0.0, scale=1.0)
                r.activation(out=A8f[:, :], in_=A6f[:, 0:2048], func=AF.Square, bias=0.0, scale=1.0)
            esq = p.run("sc", wait=[ev_qp], build=sqq)
            def mmn(r):
                for tt in range(TT):
                    r.matmul(psb[4][:, ts(tt, 2)], A7f[:, ts(tt, 128)], bmask2[:, :],
                             start=True, stop=True)
                    r.matmul(psb[5][:, ts(tt, 2)], A8f[:, ts(tt, 128)], bmask2[:, :],
                             start=True, stop=True)
            evm = p.run("te", wait=[esq], build=mmn)
            def scl(r, hp=hp):
                for (bank, dst) in ((4, scq), (5, sckv)):
                    out_ap = dst[:, 2 * hp * TT: 2 * (hp + 1) * TT].rearrange(
                        "p (j tt) -> p j tt", j=2)
                    in_ap = psb[bank][:, 0:32].rearrange("p (tt j) -> p j tt", tt=TT)
                    r.activation(out=out_ap, in_=in_ap, func=AF.Exp,
                                 bias=cols[:, 2:3], scale=-0.5)
            ev_scl = p.run("sc", wait=[evm], build=scl)
            for hh in range(2):
                h = 2 * hp + hh
                off = 64 * hh
                qsl = A5f[off:off + 64, :].rearrange("p (tc t) -> p (tc t)", tc=4)
                ksl = A6f[off:off + 64, :]
                ev_p = []
                for tg in range(4):
                    def mmk(r, tg=tg, off=off):
                        for j in range(4):
                            tt = 4 * tg + j
                            r.matmul(psb[4 + j][:, 0:256], A6f[off:off + 64, ts(tt, 128)],
                                     rfsb[off:off + 64, :], start=True, stop=True)
                    evm2 = p.run("te", wait=[ev_scl, ev_epi_prev] + (ev_p[-1:] if ev_p else []), build=mmk)
                    def rrk(r, tg=tg):
                        for j in range(4):
                            r.tensor_scalar(out=A7f[:, ts(j, 256)], in0=psb[4 + j][:, 0:256],
                                            scalar1=INV2PI, scalar2=MAGIC, op0=ALU.mult, op1=ALU.add)
                            r.tensor_scalar(out=A7f[:, ts(j, 256)], in0=A7f[:, ts(j, 256)],
                                            scalar1=MAGIC, scalar2=None, op0=ALU.subtract)
                            r.scalar_tensor_tensor(out=A6f[:, 2048 + j * 256: 2048 + (j + 1) * 256],
                                                   in0=A7f[:, ts(j, 256)],
                                                   scalar=-2 * PI, in1=psb[4 + j][:, 0:256],
                                                   op0=ALU.mult, op1=ALU.add)
                    evr = p.run("ve", wait=[evm2], build=rrk)
                    def snk(r, tg=tg):
                        for j in range(4):
                            tt = 4 * tg + j
                            rs = A6f[:, 2048 + j * 256: 2048 + (j + 1) * 256]
                            r.activation(out=A7f[:, ts(j, 256)], in_=rs,
                                         func=AF.Abs, bias=0.0, scale=1.0)
                            r.activation(out=A0b[:, KFb + tt * 512: KFb + tt * 512 + 256],
                                         in_=A7f[:, ts(j, 256)], func=AF.Sin,
                                         bias=cols[:, 0:1], scale=-1.0)
                            r.activation(out=A0b[:, KFb + tt * 512 + 256: KFb + tt * 512 + 512],
                                         in_=rs, func=AF.Sin, bias=0.0, scale=1.0)
                    ev_p.append(p.run("sc", wait=[evr], build=snk))
                ev_vh = p.run("gp", wait=[ev_vall, ev_rhs_prev], build=lambda r, h=h: r.dma_start(
                    out=A4b[:, 2048:3072].rearrange("p (tt e) -> p tt e", tt=TT),
                    in_=vT_d[h].rearrange("(tt p) e -> p tt e", p=128)))
                def mkrhs(r, h=h):
                    for tt in range(TT):
                        b0 = 3072 + tt * 65
                        r.tensor_scalar(out=A4b[:, b0: b0 + 64],
                                        in0=A4b[:, 2048 + tt * DH: 2048 + (tt + 1) * DH],
                                        scalar1=sckv[:, h * TT + tt: h * TT + tt + 1],
                                        scalar2=None, op0=ALU.mult)
                        r.tensor_copy(A4b[:, b0 + 64: b0 + 65],
                                      sckv[:, h * TT + tt: h * TT + tt + 1])
                ev_rhs = p.run("ve", wait=[ev_vh, ev_scl, ev_p[-1]], build=mkrhs)
                ev_rhs_prev = ev_rhs
                def mmkv(r):
                    for tt in range(TT):
                        for mc in range(4):
                            r.matmul(psb[mc][:, 0:65],
                                     A0b[:, KFb + tt * 512 + mc * 128: KFb + tt * 512 + (mc + 1) * 128],
                                     A4b[:, 3072 + tt * 65: 3072 + tt * 65 + 65],
                                     start=(tt == 0), stop=(tt == TT - 1))
                ev_kv = p.run("te", wait=[ev_rhs, ev_p[-1]], build=mmkv)
                def evkv(r, h=h):
                    for mc in range(4):
                        r.tensor_copy(KVs[:, h * 264 + mc * 66: h * 264 + mc * 66 + 65],
                                      psb[mc][:, 0:65])
                ev_kve = p.run("ve", wait=[ev_kv], build=evkv)
                ev_p2 = []
                for mc in range(2):
                    def mmq(r, mc=mc, off=off):
                        for tc in range(4):
                            r.matmul(psb[4 + tc][:, 0:512], rfsb[off:off + 64, ts(mc, 128)],
                                     A5f[off:off + 64, ts(tc, 512)], start=True, stop=True)
                    evm3 = p.run("te", wait=[ev_kve] + (ev_p2[-1:] if ev_p2 else []), build=mmq)
                    def rrq(r, mc=mc):
                        for tc in range(4):
                            r.tensor_scalar(out=A7f[:, ts(tc, 512)], in0=psb[4 + tc][:, 0:512],
                                            scalar1=INV2PI, scalar2=MAGIC, op0=ALU.mult, op1=ALU.add)
                            r.tensor_scalar(out=A7f[:, ts(tc, 512)], in0=A7f[:, ts(tc, 512)],
                                            scalar1=MAGIC, scalar2=None, op0=ALU.subtract)
                            r.scalar_tensor_tensor(out=A6f[:, 2048 + tc * 512: 2048 + (tc + 1) * 512],
                                                   in0=A7f[:, ts(tc, 512)],
                                                   scalar=-2 * PI, in1=psb[4 + tc][:, 0:512],
                                                   op0=ALU.mult, op1=ALU.add)
                    evr2 = p.run("ve", wait=[evm3], build=rrq)
                    def snq(r, mc=mc):
                        rs = A6f[:, 2048:4096]
                        r.activation(out=A7f[:, :], in_=rs, func=AF.Abs, bias=0.0, scale=1.0)
                        r.activation(out=A0b[:, mc * 2048: (mc + 1) * 2048], in_=A7f[:, :],
                                     func=AF.Sin, bias=cols[:, 0:1], scale=-1.0)
                        r.activation(out=A0b[:, 4096 + mc * 2048: 4096 + (mc + 1) * 2048],
                                     in_=rs, func=AF.Sin, bias=0.0, scale=1.0)
                    ev_p2.append(p.run("sc", wait=[evr2], build=snq))
                ev_round = []
                for tg in range(4):
                    def mmqkv(r, tg=tg, h=h):
                        for j in range(4):
                            tt = 4 * tg + j
                            for c in range(4):
                                r.matmul(psb[4 + j][:, 0:65],
                                         A0b[:, (c % 2) * 2048 + (c // 2) * 4096 + tt * 128:
                                                (c % 2) * 2048 + (c // 2) * 4096 + (tt + 1) * 128],
                                         KVs[:, h * 264 + c * 66: h * 264 + c * 66 + 65],
                                         start=(c == 0), stop=(c == 3))
                    evm4 = p.run("te", wait=[ev_p2[-1]] + (ev_round[-1:] if ev_round else []), build=mmqkv)
                    def epi(r, tg=tg, h=h, hh=hh):
                        for j in range(4):
                            tt = 4 * tg + j
                            r.tensor_scalar(out=lnst[:, 0:1], in0=psb[4 + j][:, 64:65],
                                            scalar1=scq[:, h * TT + tt: h * TT + tt + 1],
                                            scalar2=1e-6, op0=ALU.mult, op1=ALU.max)
                            r.reciprocal(lnst[:, 1:2], lnst[:, 0:1])
                            r.drain()
                            r.tensor_tensor(out=lnst[:, 2:3], in0=lnst[:, 1:2],
                                            in1=scq[:, h * TT + tt: h * TT + tt + 1], op=ALU.mult)
                            r.drain()
                            r.tensor_scalar(out=A8f[:, tt * 128 + 64 * hh: tt * 128 + 64 * hh + 64],
                                            in0=psb[4 + j][:, 0:64],
                                            scalar1=lnst[:, 2:3], scalar2=None, op0=ALU.mult)
                    ev_round.append(p.run("ve", wait=[evm4], build=epi))
                ev_epi_prev = ev_round[-1]
                if hh == 1:
                    ev_aoh = p.run("gp", wait=ev_round + ev_round_prev, build=lambda r, hp=hp: r.dma_start(
                        out=aoP_d[hp].rearrange("(tt p) e -> p tt e", p=128),
                        in_=A8f[:, :].rearrange("p (tt e) -> p tt e", tt=TT)))
                    ev_h_last = ev_aoh
                    ev_ao.append(ev_aoh)
                else:
                    ev_round_prev = ev_round
                    ev_h_last = ev_round[-1]
        if STAGE <= 4:
            return ev_h_last
        evw_full = p.run("gp", wait=[ev_h_last], build=lambda r: r.dma_start(
            out=A1b[:, 0:8192].rearrange("p (fc e) -> p fc e", fc=DC),
            in_=woT.rearrange("(fc p) e -> p fc e", p=128)))
        ev_acc = []
        for rnd in range(2):
            ev_aof = []
            for j in range(4):
                hp = 4 * rnd + j
                ev_aof.append(p.run("sy", wait=[ev_h_last] + ev_acc + (ev_aof[-1:] if ev_aof else []),
                    build=lambda r, hp=hp, j=j:
                    r.dma_start_transpose(out=A0b[:, ts(j, 2048)], in_=aoP_d[hp])))
            ev_rnd = []
            for tg in range(4):
                def mmo(r, tg=tg, rnd=rnd):
                    for j in range(4):
                        tt = 4 * tg + j
                        for ec2 in range(2):
                            for fc4 in range(4):
                                fc = 4 * rnd + fc4
                                r.matmul(psb[2 * j + ec2][:, 0:512],
                                         A0b[:, :].rearrange("p (fc t) -> p fc t", fc=4)[:, fc4, ts(tt, 128)],
                                         A1b[:, fc * 1024 + ec2 * 512: fc * 1024 + (ec2 + 1) * 512],
                                         start=(fc4 == 0), stop=(fc4 == 3))
                evm5 = p.run("te", wait=[evw_full] + ev_aof + (ev_rnd[-1:] if ev_rnd else []), build=mmo)
                def acc(r, tg=tg, rnd=rnd):
                    for j in range(4):
                        tt = 4 * tg + j
                        for ec2 in range(2):
                            sl = xres[:, tt * D + ec2 * 512: tt * D + (ec2 + 1) * 512]
                            r.tensor_tensor(out=sl, in0=sl, in1=psb[2 * j + ec2][:, 0:512],
                                            op=ALU.add)
                ev_rnd.append(p.run("ve", wait=[evm5], build=acc))
            ev_acc = ev_rnd
        return ev_acc[-1]

    junkv = msk

    def ffn(ev_prev, tag):
        ev_xnw, lnhalves = apply_ln([ev_prev] + boot, affine=False)
        ev_xnf = read_xnF(lnhalves[1], A1b)
        ev_c = ev_cols
        ev_kkh = p.run("gp", wait=[], build=lambda r: r.dma_start(out=kkh[:, :], in_=kkrep[:, :]))
        evwg = p.run("gp", wait=[ev_prev], build=lambda r: r.dma_start(
            out=A4b[:, 0:128].rearrange("p (dc e) -> p dc e", dc=DC),
            in_=WgW.rearrange("(dc p) e -> p dc e", p=128)))
        def mml(r):
            for tc in range(4):
                for dc2 in range(DC):
                    r.matmul(psb[tc][0:16, 0:512],
                             A4b[:, dc2 * 16: (dc2 + 1) * 16],
                             A1b[:, dc2 * T + tc * 512: dc2 * T + tc * 512 + 512],
                             start=(dc2 == 0), stop=(dc2 == DC - 1))
        evml = p.run("te", wait=[evwg] + ev_xnf, build=mml)
        def evl(r):
            for tc in range(4):
                r.tensor_scalar(out=lg[:, ts(tc, 512)], in0=psb[tc][0:16, 0:512],
                                scalar1=bgc[:, 0:1], scalar2=None, op0=ALU.add)
        ev_lg = p.run("ve", wait=[evml, ev_c], build=evl)
        def smax(r):
            r.reduce_max(thrc[:, 0:1], lg[:, :], axis=AXX)
            r.tensor_scalar(out=thrc[:, 1:2], in0=thrc[:, 0:1], scalar1=-1.0,
                            scalar2=None, op0=ALU.mult)
        ev_m = p.run("ve", wait=[ev_lg], build=smax)
        ev_e = p.run("sc", wait=[ev_m], build=lambda r: r.activation(
            out=lg[:, :], in_=lg[:, :], func=AF.Exp, bias=thrc[0:16, 1:2], scale=1.0))
        def sden(r):
            r.reduce_sum(thrc[:, 2:3], lg[:, :], axis=AXX)
            r.drain()
            r.reciprocal(thrc[:, 3:4], thrc[:, 2:3])
            r.drain()
            r.tensor_scalar(out=lg[:, :], in0=lg[:, :], scalar1=thrc[0:16, 3:4],
                            scalar2=None, op0=ALU.mult)
        ev_pr = p.run("ve", wait=[ev_e], build=sden)
        ev_pb = p.run("ve", wait=[ev_pr], build=lambda r: r.tensor_copy(
            A0b[0:16, 0:2048], lg[:, :]))
        ev_ptr = []
        for tt2 in range(TT):
            ev_ptr.append(p.run("sy", wait=[ev_pb, ev_xnw] + (ev_ptr[-1:] if ev_ptr else []),
                build=lambda r, tt2=tt2: r.dma_start_transpose(
                    out=A0b[:, 2048 + tt2 * 16: 2048 + (tt2 + 1) * 16],
                    in_=A0b[0:16, tt2 * 128: (tt2 + 1) * 128])))
        ev_pw = p.run("sy", wait=ev_ptr, build=lambda r: r.dma_start(
            out=xnT_d[:, D:D + NE].rearrange("(tt p) e -> p tt e", p=128),
            in_=A0b[:, 2048:2048 + 256].rearrange("p (tt e) -> p tt e", tt=TT)))
        ev_b = p.run("ve", wait=[ev_pr], build=lambda r: (r.memset(thrc[:, 4:5], 0.0),
                                                          r.memset(thrc[:, 5:6], 0.5)))
        evi = ev_b
        for it in range(NBIS):
            stepv = 0.5 ** (it + 1)
            def bis(r, stepv=stepv):
                r.tensor_scalar(out=thrc[:, 6:7], in0=thrc[:, 4:5], scalar1=stepv,
                                scalar2=None, op0=ALU.add)
                r.drain()
                r.tensor_scalar(out=junkv[:, :], in0=lg[:, :], scalar1=thrc[0:16, 6:7],
                                scalar2=None, op0=ALU.is_gt, op1=ALU.add,
                                accum_out=thrc[:, 7:8])
                r.drain()
                r.tensor_scalar(out=thrc[:, 6:7], in0=thrc[:, 7:8], scalar1=255.5,
                                scalar2=None, op0=ALU.is_gt)
                r.drain()
                r.scalar_tensor_tensor(out=thrc[:, 4:5], in0=thrc[:, 6:7], scalar=stepv,
                                       in1=thrc[:, 4:5], op0=ALU.mult, op1=ALU.add)
            evi = p.run("ve", wait=[evi], build=bis)
        def mkmask(r):
            r.tensor_scalar(out=msk[:, :], in0=lg[:, :], scalar1=thrc[0:16, 4:5],
                            scalar2=None, op0=ALU.is_gt)
        ev_msk = p.run("ve", wait=[evi], build=mkmask)
        def cums3(r):
            r.tensor_tensor_scan(out=pos[:, 0:512], data0=onef[:, :],
                                 data1=msk[:, 0:512], initial=0.0, op0=ALU.mult, op1=ALU.add)
            for c4 in range(1, 4):
                r.drain()
                r.tensor_tensor(out=msk[:, c4 * 512: c4 * 512 + 1],
                                in0=msk[:, c4 * 512: c4 * 512 + 1],
                                in1=pos[:, c4 * 512 - 1: c4 * 512], op=ALU.add)
                r.drain()
                r.tensor_tensor_scan(out=pos[:, c4 * 512: (c4 + 1) * 512],
                                     data0=onef[:, :], data1=msk[:, c4 * 512: (c4 + 1) * 512],
                                     initial=0.0, op0=ALU.mult, op1=ALU.add)
        ev_pos = p.run("ve", wait=[ev_msk], build=cums3)
        ev_pf = p.run("ve", wait=[ev_pos], build=lambda r: r.tensor_copy(
            posh[:, :], pos[:, :]))
        ev_pt = []
        for tt2 in range(TT):
            ev_pt.append(p.run("sy", wait=[ev_pf, ev_pw] + (ev_pt[-1:] if ev_pt else []),
                build=lambda r, tt2=tt2: r.dma_start_transpose(
                    out=posT[:, tt2 * 16: (tt2 + 1) * 16],
                    in_=posh[:, tt2 * 128: (tt2 + 1) * 128])))
        ev_idx = []
        for e in range(NE):
            def mkR(r, e=e):
                for tt2 in range(TT):
                    r.tensor_tensor(out=A0b[:, tt2 * 256: (tt2 + 1) * 256],
                                    in0=posT[:, tt2 * 16 + e: tt2 * 16 + e + 1].to_broadcast([128, 256]),
                                    in1=kkh[:, :], op=ALU.is_ge)
            evR = p.run("ve", wait=ev_pt + (ev_idx[-1:] if ev_idx else []), build=mkR)
            def mmR(r, e=e):
                for half in range(2):
                    for tt2 in range(TT):
                        r.matmul(psb[6][:, half: half + 1],
                                 A0b[:, tt2 * 256 + half * 128: tt2 * 256 + (half + 1) * 128],
                                 onesb[:, :], start=(tt2 == 0), stop=(tt2 == TT - 1))
            evM = p.run("te", wait=[evR], build=mmR)
            def mkidx(r, e=e):
                for half in range(2):
                    r.tensor_scalar(out=idxf[:, 2 * e + half: 2 * e + half + 1],
                                    in0=psb[6][:, half: half + 1], scalar1=-1.0,
                                    scalar2=float(T), op0=ALU.mult, op1=ALU.add)
                r.drain()
                r.tensor_copy(idxc[:, 2 * e: 2 * e + 2], idxf[:, 2 * e: 2 * e + 2])
            ev_idx.append(p.run("ve", wait=[evM], build=mkidx))
        ev_g = []
        gw_l, gv_l = {}, {}
        for e in range(NE):
            for half in range(2):
                kq = 2 * e + half
                evg = p.run("gp", wait=[ev_idx[e], ev_pw] +
                            [x for x in (gw_l.get(kq - 2), gv_l.get(kq - 2)) if x],
                    build=lambda r, e=e, half=half: r.indirect_dma_start(
                        out=A0b[:, 4096:4096 + 1040] if half == 0 else A0b[:, 5136:5136 + 1040],
                        out_offset=None, in_=xnT_d[:, :],
                        in_offset=IndirectOffsetOnAxis(ap=idxc[:, 2 * e + half: 2 * e + half + 1], axis=0)))
                ev_g.append(evg)
                evw2 = p.run("sy", wait=[evg], build=lambda r, e=e, half=half: r.dma_start(
                    out=gat_d[e * K + half * 128: e * K + (half + 1) * 128, :].rearrange(
                        "(q p) c -> p q c", p=128),
                    in_=(A0b[:, 4096:4096 + 1040] if half == 0 else A0b[:, 5136:5136 + 1040]).rearrange(
                        "p (q c) -> p q c", q=1)))
                ev_g.append(evw2)
                gw_l[kq] = evw2
                evv2 = p.run("ve", wait=[evg], build=lambda r, e=e, half=half: r.tensor_copy(
                    valsb[:, 2 * e + half: 2 * e + half + 1],
                    (A0b[:, 4096:4096 + 1040] if half == 0 else A0b[:, 5136:5136 + 1040])[:, D + e: D + e + 1]))
                ev_g.append(evv2)
                gv_l[kq] = evv2
        ev_gall = ev_g[-1]
        ev_sc_list = []
        ev_acc_d = None
        ev_z0 = p.run("ve", wait=[ev_prev], build=lambda r: r.memset(A8f[:, 0:1024], 0.0))
        def zacc(r):
            for tt2 in range(TT):
                r.dma_start(out=acc_d[ts(tt2, 128), :], in_=A8f[:, 0:1024])
        ev_zall = p.run("sy", wait=[ev_z0], build=zacc)
        ev_h3w = []
        A5bf = A5f[:, :].bitcast(BF16)
        wslab = lambda k: A1b[:, 8192:16384] if k % 2 == 0 else A5bf[:, 0:8192]
        ev_ld = {}
        ev_mm = {}
        ev_ev = {}
        ev_tr_g = {}

        def mk_wload(k):
            e2, li2 = divmod(k, 3)
            wmat = (w1i, w2i, w3i)[li2]
            war = ev_mm.get(k - 2, ev_gall)
            ev_ld[k] = p.run("gp", wait=[war],
                build=lambda r, e2=e2, wmat=wmat, k=k: r.dma_start(
                    out=wslab(k).rearrange("p (dc o) -> p dc o", dc=DC),
                    in_=wmat[e2].rearrange("(dc p) o -> p dc o", p=128)))

        mk_wload(0)
        mk_wload(1)
        for g4 in range(4):
            ev_tr = []
            for dc2 in range(DC):
                ev_tr.append(p.run("sy", wait=[ev_gall] + (ev_tr[-1:] if ev_tr else []) +
                                   ([ev_mm[3 * (4 * g4 - 1)]] if g4 > 0 else []),
                    build=lambda r, g4=g4, dc2=dc2: r.dma_start_transpose(
                        out=A1b[:, dc2 * 1024: (dc2 + 1) * 1024],
                        in_=gat_d[g4 * 1024: (g4 + 1) * 1024, ts(dc2, 128)])))
            ev_tr_g[g4] = ev_tr
            for e4 in range(4):
                e = 4 * g4 + e4
                for li in range(3):
                    k = 3 * e + li
                    pbase = 0 if k % 2 == 0 else 4
                    sl = wslab(k)
                    def mme(r, li=li, e4=e4, sl=sl, pbase=pbase):
                        for oc in range(DC):
                            for dc2 in range(DC):
                                if li == 0:
                                    rhs = A1b[:, dc2 * 1024 + e4 * 256: dc2 * 1024 + e4 * 256 + 256]
                                elif li == 1:
                                    rhs = A0b[:, dc2 * 256: (dc2 + 1) * 256]
                                else:
                                    rhs = A0b[:, 2048 + dc2 * 256: 2048 + (dc2 + 1) * 256]
                                r.matmul(psb[pbase + oc // 2][:, (oc % 2) * 256: (oc % 2) * 256 + 256],
                                         sl[:, dc2 * 1024 + oc * 128: dc2 * 1024 + (oc + 1) * 128],
                                         rhs, start=(dc2 == 0), stop=(dc2 == DC - 1))
                    mw = [ev_ld[k], ev_ev.get(k - 2)] + \
                         ([ev_ev[k - 1]] if li > 0 else ev_tr)
                    ev_mm[k] = p.run("te", wait=[e for e in mw if e], build=mme)
                    if k + 2 < 3 * NE:
                        mk_wload(k + 2)
                    if li < 2:
                        def eve6(r, li=li, e=e, pbase=pbase):
                            for oc in range(DC):
                                src2 = psb[pbase + oc // 2][:, (oc % 2) * 256: (oc % 2) * 256 + 256]
                                r.activation(out=A0b[:, li * 2048 + oc * 256: li * 2048 + (oc + 1) * 256],
                                             in_=src2, func=AF.Relu,
                                             bias=bexp[:, li * NE + e: li * NE + e + 1], scale=1.0)
                        ev_ev[k] = p.run("sc", wait=[ev_mm[k]], build=eve6)
                    else:
                        def eve6(r, e=e, pbase=pbase):
                            for oc in range(DC):
                                src2 = psb[pbase + oc // 2][:, (oc % 2) * 256: (oc % 2) * 256 + 256]
                                r.tensor_scalar(out=A0b[:, 4096 + oc * 256: 4096 + (oc + 1) * 256],
                                                in0=src2, scalar1=bexp[:, 2 * NE + e: 2 * NE + e + 1],
                                                scalar2=None, op0=ALU.add)
                        ev_ev[k] = p.run("ve", wait=[ev_mm[k]] + (ev_h3w[-1:] if ev_h3w else []),
                                         build=eve6)
                    ev_sc_list.append(ev_ev[k])
                ev_h3w.append(p.run("sy", wait=[ev_sc_list[-1]] + (ev_h3w[-1:] if ev_h3w else []),
                    build=lambda r, e=e: r.dma_start(
                        out=h3F_d[:, e * K: (e + 1) * K].rearrange("(oc p) k -> p oc k", p=128),
                        in_=A0b[:, 4096:6144].rearrange("p (oc k) -> p oc k", oc=DC))))
        ev_s = []
        st_l, sv_l, ss_l = {}, {}, {}
        for e in range(NE):
            for half in range(2):
                kq = 2 * e + half
                sl = e * K + half * 128
                tb = A0b[:, 6144:7168] if kq % 2 == 0 else A0b[:, 4096:5120]
                sb2 = A0b[:, 7168:8192] if kq % 2 == 0 else A0b[:, 5120:6144]
                evt = p.run("sy", wait=[ev_h3w[-1], ev_zall] +
                            [x for x in (sv_l.get(kq - 2),) if x],
                    build=lambda r, sl=sl, tb=tb: r.dma_start_transpose(
                        out=tb, in_=h3F_d[:, sl:sl + 128]))
                evv3 = p.run("ve", wait=[evt] + [x for x in (ss_l.get(kq - 2),) if x],
                    build=lambda r, e=e, half=half, tb=tb, sb2=sb2: r.tensor_scalar(
                        out=sb2, in0=tb,
                        scalar1=valsb[:, 2 * e + half: 2 * e + half + 1], scalar2=None,
                        op0=ALU.mult))
                evs3 = p.run("gp", wait=[evv3] + [x for x in (ss_l.get(kq - 1),) if x],
                    build=lambda r, e=e, half=half, sb2=sb2: r.indirect_dma_start(
                        out=acc_d[:, :], out_offset=IndirectOffsetOnAxis(
                            ap=idxc[:, 2 * e + half: 2 * e + half + 1], axis=0),
                        in_=sb2, in_offset=None, compute_op=ALU.add))
                st_l[kq], sv_l[kq], ss_l[kq] = evt, evv3, evs3
                ev_s += [evt, evv3, evs3]
        ev_fin = []
        for tq in range(4):
            evr4 = p.run("sy", wait=[ev_s[-1]] + (ev_fin[-1:] if ev_fin else []),
                build=lambda r, tq=tq: r.dma_start(
                    out=A5f[:, :].rearrange("p (tt d) -> p tt d", tt=4),
                    in_=acc_d[tq * 512:(tq + 1) * 512, :].rearrange("(tt p) d -> p tt d", p=128)))
            def adds(r, tq=tq):
                for j in range(4):
                    tt2 = 4 * tq + j
                    sl = xres[:, ts(tt2, D)]
                    r.tensor_tensor(out=sl, in0=sl, in1=A5f[:, ts(j, D)], op=ALU.add)
            eva4 = p.run("ve", wait=[evr4], build=adds)
            ev_fin += [evr4, eva4]
        return p.run("gp", wait=[ev_fin[-1], ev_ld[3 * NE - 1]],
                     build=lambda r: r.nop())

    hstate = S("hstate", [128, DIT * NS], F32)

    def double_ln(wait_evs):
        out_evs, half_evs, prev = [], [], None
        for tt in range(TT):
            xt = xres[:, ts(tt, D)]
            e1 = p.run("ve", wait=wait_evs, build=lambda r, xt=xt: r.reduce_sum(
                lnst[:, 0:1], xt, axis=AXX))
            e2 = p.run("sc", wait=wait_evs + [prev], build=lambda r, xt=xt: r.activation(
                out=A9f, in_=xt, func=AF.Square, bias=0.0, scale=1.0))
            e3 = p.run("ve", wait=[e2], build=lambda r: r.reduce_sum(
                lnst[:, 1:2], A9f, axis=AXX))
            def stats(r):
                r.mul(lnst[:, 2:3], lnst[:, 0:1], 1.0 / D)
                r.drain()
                r.activation(out=lnst[:, 3:4], in_=lnst[:, 2:3], func=AF.Square,
                             bias=0.0, scale=1.0)
            e4 = p.run("sc", wait=[e1, e3], build=stats)
            e5 = p.run("ve", wait=[e4], build=lambda r: r.scalar_tensor_tensor(
                out=lnst[:, 4:5], in0=lnst[:, 1:2], scalar=1.0 / D,
                in1=lnst[:, 3:4], op0=ALU.mult, op1=ALU.subtract))
            def invf(r):
                r.activation(out=lnst[:, 5:6], in_=lnst[:, 4:5], func=AF.Ln,
                             bias=cols[:, 3:4], scale=1.0)
                r.drain()
                r.activation(out=lnst[:, 6:7], in_=lnst[:, 5:6], func=AF.Exp,
                             bias=0.0, scale=-0.5)
            e6 = p.run("sc", wait=[e5], build=invf)
            def xxf(r, xt=xt):
                r.tensor_scalar(out=A9f, in0=xt, scalar1=lnst[:, 2:3],
                                scalar2=lnst[:, 6:7], op0=ALU.subtract, op1=ALU.mult)
                r.drain()
                r.tensor_tensor(out=xt, in0=xt, in1=A9f, op=ALU.add)
            e7 = p.run("ve", wait=[e6], build=xxf)
            e8 = p.run("ve", wait=[e7], build=lambda r: r.reduce_sum(
                lnst[:, 0:1], A9f, axis=AXX))
            e9 = p.run("sc", wait=[e7], build=lambda r: r.activation(
                out=A5f[:, 1024:2048], in_=A9f, func=AF.Square, bias=0.0, scale=1.0))
            e10 = p.run("ve", wait=[e9], build=lambda r: r.reduce_sum(
                lnst[:, 1:2], A5f[:, 1024:2048], axis=AXX))
            e11 = p.run("sc", wait=[e8, e10], build=stats)
            e12 = p.run("ve", wait=[e11], build=lambda r: r.scalar_tensor_tensor(
                out=lnst[:, 4:5], in0=lnst[:, 1:2], scalar=1.0 / D,
                in1=lnst[:, 3:4], op0=ALU.mult, op1=ALU.subtract))
            e13 = p.run("sc", wait=[e12], build=invf)
            dst = A0b[:, ts(tt % 8, D)]
            def aff2(r, dst=dst):
                r.tensor_scalar(out=A5f[:, 1024:2048], in0=A9f, scalar1=lnst[:, 2:3],
                                scalar2=lnst[:, 6:7], op0=ALU.subtract, op1=ALU.mult)
                r.tensor_mul(A5f[:, 1024:2048], A5f[:, 1024:2048], gam[:, :])
                r.tensor_tensor(out=dst, in0=A5f[:, 1024:2048], in1=bet[:, :], op=ALU.add)
            e14 = p.run("ve", wait=[e13, half_evs[-1] if tt >= 8 and half_evs else None],
                        build=aff2)
            prev = e14
            out_evs.append(e14)
            if tt % 8 == 7:
                hw = p.run("sy", wait=out_evs[-8:], build=lambda r, tt=tt: r.dma_start(
                    out=xnT_d[(tt - 7) * 128:(tt + 1) * 128, 0:D].rearrange(
                        "(t8 p) d -> p t8 d", p=128),
                    in_=A0b[:, :].rearrange("p (t8 d) -> p t8 d", t8=8)))
                half_evs.append(hw)
        return half_evs[-1], half_evs

    def attn_b(ev_prev):
        r0 = p.run("sy", wait=[ev_prev] + p.bar()[:0], build=lambda r: (
            r.dma_start(out=A6f[0:1, 0:1024], in_=ln_gi[:, :]),
            r.dma_start(out=A6f[0:1, 1024:2048], in_=ln_bi[:, :])))
        def bc2(r):
            r.matmul(psb[0][:, 0:512], onesc[:, :], A6f[0:1, 0:512], start=True, stop=True)
            r.matmul(psb[1][:, 0:512], onesc[:, :], A6f[0:1, 512:1024], start=True, stop=True)
            r.matmul(psb[2][:, 0:512], onesc[:, :], A6f[0:1, 1024:1536], start=True, stop=True)
            r.matmul(psb[3][:, 0:512], onesc[:, :], A6f[0:1, 1536:2048], start=True, stop=True)
        eb = p.run("te", wait=[r0, ev_prev], build=bc2)
        def bc3(r):
            r.tensor_copy(gam[:, 0:512], psb[0][:, 0:512])
            r.tensor_copy(gam[:, 512:1024], psb[1][:, 0:512])
            r.tensor_copy(bet[:, 0:512], psb[2][:, 0:512])
            r.tensor_copy(bet[:, 512:1024], psb[3][:, 0:512])
        eb2 = p.run("ve", wait=[eb], build=bc3)
        ev_xnw, lnhalves = double_ln([ev_prev, eb2] + boot)
        ev_xnf = read_xnF(lnhalves[1], A1b)
        evxp = p.run("gp", wait=[ev_prev], build=lambda r: r.dma_start(
            out=A4b[:, 3072:4352].rearrange("p (ic c) -> p ic c", ic=DIT),
            in_=x_projW.rearrange("(ic p) c -> p ic c", p=128)))
        xb_d = dint("xb_d", [DI, T], BF16) if "xb_d" not in dd else dd["xb_d"]
        ev_ic = []
        ev_w_prev = None
        gamb = gam[:, 0:512].bitcast(BF16)
        wsl = lambda k: A4b[:, 0:1024] if k % 2 == 0 else gamb
        evwl, evmmi, evevi, evcvi, evsli, evxpi, evwri = {}, {}, {}, {}, {}, {}, {}

        def iload(k, woff=0):
            war = evmmi.get(k - 2)
            evwl[k] = p.run("gp", wait=[war if war else ev_xnw],
                build=lambda r, k=k: r.dma_start(
                    out=wsl(k).rearrange("p (dc e) -> p dc e", dc=DC),
                    in_=in_projW[:, woff + k * 128: woff + (k + 1) * 128].rearrange(
                        "(dc p) e -> p dc e", p=128)))

        iload(0)
        iload(1)
        for ic in range(DIT):
            def mmip(r, ic=ic):
                sl = wsl(ic)
                for tc in range(4):
                    for dc2 in range(DC):
                        r.matmul(psb[tc][:, 0:512], sl[:, ts(dc2, 128)],
                                 A1b[:, dc2 * T + tc * 512: dc2 * T + tc * 512 + 512],
                                 start=(dc2 == 0), stop=(dc2 == DC - 1))
            evmmi[ic] = p.run("te", wait=[evwl[ic]] + ev_xnf +
                              ([evevi[ic - 1]] if ic else []), build=mmip)
            if ic >= 1:
                def mmxp(r, ic2=ic - 1):
                    for tc in range(4):
                        r.matmul(psb[4 + tc][0:80, 0:512],
                                 A4b[:, 3072 + ic2 * 80: 3072 + (ic2 + 1) * 80],
                                 A4b[:, 1024 + tc * 512: 1024 + (tc + 1) * 512],
                                 start=(ic2 == 0), stop=(ic2 == DIT - 1))
                evxpi[ic - 1] = p.run("te", wait=[evsli[ic - 1], evxp], build=mmxp)
            if ic + 2 < DIT:
                iload(ic + 2)
            def evac(r):
                for tc in range(4):
                    r.tensor_copy(A7f[:, ts(tc, 512)], psb[tc][:, 0:512])
            evevi[ic] = p.run("ve", wait=[evmmi[ic]] + ([evcvi[ic - 1]] if ic else []),
                              build=evac)
            def conv(r, ic=ic):
                r.tensor_scalar(out=A8f[:, :], in0=A7f[:, :],
                                scalar1=cwc[:, ic * 4 + 3: ic * 4 + 4], scalar2=None, op0=ALU.mult)
                r.scalar_tensor_tensor(out=A8f[:, 1:2048], in0=A7f[:, 0:2047],
                                       scalar=cwc[:, ic * 4 + 2: ic * 4 + 3], in1=A8f[:, 1:2048],
                                       op0=ALU.mult, op1=ALU.add)
                r.scalar_tensor_tensor(out=A8f[:, 2:2048], in0=A7f[:, 0:2046],
                                       scalar=cwc[:, ic * 4 + 1: ic * 4 + 2], in1=A8f[:, 2:2048],
                                       op0=ALU.mult, op1=ALU.add)
                r.scalar_tensor_tensor(out=A8f[:, 3:2048], in0=A7f[:, 0:2045],
                                       scalar=cwc[:, ic * 4 + 0: ic * 4 + 1], in1=A8f[:, 3:2048],
                                       op0=ALU.mult, op1=ALU.add)
            evcvi[ic] = p.run("ve", wait=[evevi[ic]] + ([evsli[ic - 1]] if ic else []),
                              build=conv)
            sw = [evcvi[ic]] + [e for e in (evxpi.get(ic - 1), evwri.get(ic - 1)) if e]
            evsli[ic] = p.run("sc", wait=sw, build=lambda r, ic=ic: r.activation(
                out=A4b[:, 1024:3072], in_=A8f[:, :], func=AF.Silu,
                bias=cbc[:, ic: ic + 1], scale=1.0))
            evwri[ic] = p.run("sy", wait=[evsli[ic]], build=lambda r, ic=ic: r.dma_start(
                out=xb_d[ts(ic, 128), :], in_=A4b[:, 1024:3072]))
        def mmxpL(r):
            for tc in range(4):
                r.matmul(psb[4 + tc][0:80, 0:512],
                         A4b[:, 3072 + (DIT - 1) * 80: 3072 + DIT * 80],
                         A4b[:, 1024 + tc * 512: 1024 + (tc + 1) * 512],
                         start=False, stop=True)
        evxpi[DIT - 1] = p.run("te", wait=[evsli[DIT - 1], evxp], build=mmxpL)
        ev_ic.append(p.run("gp", wait=[evxpi[DIT - 1], evwri[DIT - 1]],
                           build=lambda r: r.nop()))
        def pev(r):
            for tc in range(4):
                r.tensor_copy(A0b[0:16, tc * 512: (tc + 1) * 512], psb[4 + tc][0:16, 0:512])
                r.tensor_copy(A0b[0:16, 2048 + tc * 512: 2048 + (tc + 1) * 512],
                              psb[4 + tc][32:48, 0:512])
                r.tensor_copy(A8f[0:1, tc * 512: (tc + 1) * 512], psb[4 + tc][64:65, 0:512])
        ev_pev = p.run("ve", wait=[ev_ic[-1]], build=pev)
        def dbc(r):
            for tc in range(4):
                r.matmul(psb[4 + tc][:, 0:512], onesc[:, :], A8f[0:1, ts(tc, 512)],
                         start=True, stop=True)
        ev_db = p.run("te", wait=[ev_pev], build=dbc)
        def dbe(r):
            for tc in range(4):
                r.tensor_copy(A8f[:, ts(tc, 512)], psb[4 + tc][:, 0:512])
        ev_dbe = p.run("ve", wait=[ev_db], build=dbe)
        ev_z2 = []
        evzl, evzm, evzs, evzw_ = {}, {}, {}, {}

        def zload(k):
            war = evzm.get(k - 2, evmmi[DIT - 2 + (k % 2)])
            evzl[k] = p.run("gp", wait=[war], build=lambda r, k=k: r.dma_start(
                out=wsl(k).rearrange("p (dc e) -> p dc e", dc=DC),
                in_=in_projW[:, DI + k * 128: DI + (k + 1) * 128].rearrange(
                    "(dc p) e -> p dc e", p=128)))

        zload(0)
        zload(1)
        for ic in range(DIT):
            def mmz(r, ic=ic):
                sl = wsl(ic)
                for tc in range(4):
                    for dc2 in range(DC):
                        r.matmul(psb[tc][:, 0:512], sl[:, ts(dc2, 128)],
                                 A1b[:, dc2 * T + tc * 512: dc2 * T + tc * 512 + 512],
                                 start=(dc2 == 0), stop=(dc2 == DC - 1))
            mwz = [evzl[ic]] + [e for e in (evzs.get(ic - 1),
                                            evevi[DIT - 1] if ic == 0 else None) if e]
            evzm[ic] = p.run("te", wait=mwz, build=mmz)
            if ic + 2 < DIT:
                zload(ic + 2)
            def silz(r):
                for tc in range(4):
                    r.activation(out=A4b[:, 1024 + tc * 512: 1024 + (tc + 1) * 512],
                                 in_=psb[tc][:, 0:512], func=AF.Silu, bias=0.0, scale=1.0)
            zsw = [evzm[ic]] + ([evzw_[ic - 1]] if ic else
                                [evxpi[DIT - 1], evwri[DIT - 1]])
            evzs[ic] = p.run("sc", wait=zsw, build=silz)
            evzw_[ic] = p.run("sy", wait=[evzs[ic]], build=lambda r, ic=ic: r.dma_start(
                out=zf_d[ts(ic, 128), :], in_=A4b[:, 1024:3072]))
            ev_z2.append(evzw_[ic])
        ev_hz = p.run("gp", wait=[ev_prev], build=lambda r: r.memset(hstate[:, :], 0.0))
        QT = 4
        ev_q = [ev_z2[-1], ev_dbe, ev_hz]
        ev_out = []
        vb = lambda k: slice(k * 512, (k + 1) * 512)
        hbv = lambda k: slice(2048 + k * 512, 2048 + (k + 1) * 512)
        pb0 = A6f[:, 3584:4096]
        pb1 = A4b[:, 3072:4096].bitcast(F32)
        yv = A4b[:, 2048:3072].bitcast(F32)
        lastCHb = {0: ev_hz, 1: ev_hz, 2: ev_hz}
        lastGpHb = {0: None, 1: None, 2: None}
        ev_accd = {0: None, 1: None}
        ev_acc_prev = None
        lastV15 = {0: None, 1: None}
        lastDl = {0: None, 1: None}
        ev_yz_prev = ev_yw_prev = ev_wq_prev = None
        ev_gp_lastprod = None
        ev_s11_prev = ev_s15_prev = None
        for q in range(QT):
            evp = ev_q if q == 0 else [e for e in (ev_yz_prev, ev_gp_lastprod) if e]
            for grp in range(4):
                def bcg(r, grp=grp, q=q):
                    for j in range(4):
                        s = 4 * grp + j
                        r.matmul(psb[4 + j][:, 0:512], selb[:, ts(s, 128)],
                                 A0b[0:16, q * 512: (q + 1) * 512], start=True, stop=True)
                evb4 = p.run("te", wait=evp if isinstance(evp, list) else [evp], build=bcg)
                def bce(r, grp=grp):
                    for j in range(4):
                        s = 4 * grp + j
                        r.activation(out=A1b[:, s * 512: (s + 1) * 512], in_=psb[4 + j][:, 0:512],
                                     func=AF.Copy, bias=0.0, scale=1.0)
                evp = [p.run("sc", wait=[evb4], build=bce)]
            for grp in range(4):
                def bcg2(r, grp=grp, q=q):
                    for j in range(4):
                        s = 4 * grp + j
                        r.matmul(psb[4 + j][:, 0:512], selb[:, ts(s, 128)],
                                 A0b[0:16, 2048 + q * 512: 2048 + (q + 1) * 512],
                                 start=True, stop=True)
                evb4 = p.run("te", wait=evp, build=bcg2)
                def bce2(r, grp=grp):
                    for j in range(4):
                        s = 4 * grp + j
                        r.activation(out=A1b[:, 8192 + s * 512: 8192 + (s + 1) * 512],
                                     in_=psb[4 + j][:, 0:512], func=AF.Copy, bias=0.0, scale=1.0)
                evp = [p.run("sc", wait=[evb4], build=bce2)]
            ev_rep = evp[0]

            def mk_prep(ic2, extra_ve_wait=None):
                par2 = ic2 % 2
                do2 = par2 * 1024
                evu2 = p.run("sy", wait=[e for e in (ev_rep, ev_wq_prev) if e],
                    build=lambda r: r.dma_start(
                        out=A4b[:, 1024:1536], in_=xb_d[ts(ic2, 128), ts(q, 512)]))
                def dl(r):
                    r.tensor_scalar(out=A7f[:, do2:do2 + 512], in0=A8f[:, ts(q, 512)],
                                    scalar1=dtwc[:, ic2: ic2 + 1], scalar2=dtbc[:, ic2: ic2 + 1],
                                    op0=ALU.mult, op1=ALU.add)
                    r.tensor_scalar(out=A7f[:, do2:do2 + 512], in0=A7f[:, do2:do2 + 512],
                                    scalar1=80.0, scalar2=None, op0=ALU.min)
                evd1 = p.run("ve", wait=[e for e in (ev_rep, lastDl[par2], extra_ve_wait) if e],
                             build=dl)
                def dl2(r):
                    r.activation(out=A7f[:, do2 + 512:do2 + 1024], in_=A7f[:, do2:do2 + 512],
                                 func=AF.Exp, bias=0.0, scale=1.0)
                    r.activation(out=A7f[:, do2:do2 + 512], in_=A7f[:, do2 + 512:do2 + 1024],
                                 func=AF.Ln, bias=cols[:, 4:5], scale=1.0)
                evd2 = p.run("sc", wait=[evd1], build=dl2)
                return evu2, evd2

            def mk_wq(ic2, evd2, evu2):
                par2 = ic2 % 2
                do2 = par2 * 1024
                ev = p.run("ve", wait=[e for e in (evd2, evu2, lastV15[par2]) if e],
                    build=lambda r: r.tensor_tensor(
                        out=A7f[:, do2 + 512:do2 + 1024], in0=A7f[:, do2:do2 + 512],
                        in1=A4b[:, 1024:1536], op=ALU.mult))
                return ev

            def mk_mkg(ic2, g2, evd2, regwar):
                do2 = (ic2 % 2) * 1024
                def mkg(r):
                    for jj in range(4):
                        ss = 4 * g2 + jj
                        sl = ((4 * g2) % 8) + jj
                        r.activation(out=A5f[:, sl * 512:(sl + 1) * 512],
                                     in_=A7f[:, do2:do2 + 512], func=AF.Exp, bias=0.0,
                                     scale=Acol[:, ic2 * NS + ss: ic2 * NS + ss + 1])
                return p.run("sc", wait=[e for e in (evd2, regwar) if e], build=mkg)

            evu, evd2c = mk_prep(0)
            ev_wq = mk_wq(0, evd2c, evu)
            ev_wq_prev = ev_wq
            evda = [None] * 4
            evda[0] = mk_mkg(0, 0, evd2c, ev_s11_prev)
            evda[1] = mk_mkg(0, 1, evd2c, ev_s15_prev)
            evda_next = [None] * 4
            prep_next = None
            for ic in range(DIT):
                par = ic % 2
                do = par * 1024
                evsz = p.run("sy", wait=[e for e in (ev_rep, ev_yz_prev) if e],
                    build=lambda r, q=q, ic=ic: r.dma_start(
                        out=A4b[:, 1536:2048], in_=zf_d[ts(ic, 128), ts(q, 512)]))
                dac = lambda s2: (s2 % 8) * 512
                hcol = lambda s2: hstate[:, ic * NS + s2: ic * NS + s2 + 1]
                Bcol = lambda s2: A1b[:, s2 * 512: (s2 + 1) * 512]
                Ccol = lambda s2: A1b[:, 8192 + s2 * 512: 8192 + (s2 + 1) * 512]

                def vtt(r, s2):
                    k = s2 % 4
                    r.tensor_tensor(out=A6f[:, k * 512 + 1:(k + 1) * 512],
                                    in0=A7f[:, do + 513:do + 1024],
                                    in1=Bcol(s2)[:, 1:512], op=ALU.mult)

                def pv(r, s2):
                    k = s2 % 4
                    r.tensor_tensor(out=lnst[:, 7:8], in0=A5f[:, dac(s2):dac(s2) + 1],
                                    in1=hcol(s2), op=ALU.mult)
                    r.tensor_tensor(out=A6f[:, k * 512: k * 512 + 1],
                                    in0=A7f[:, do + 512:do + 513],
                                    in1=Bcol(s2)[:, 0:1], op=ALU.mult)
                    r.tensor_tensor(out=A6f[:, k * 512: k * 512 + 1],
                                    in0=A6f[:, k * 512: k * 512 + 1],
                                    in1=lnst[:, 7:8], op=ALU.add)

                def scn(r, s2):
                    r.tensor_tensor_scan(out=A6f[:, hbv(s2 % 3)],
                                         data0=A5f[:, dac(s2):dac(s2) + 512],
                                         data1=A6f[:, vb(s2 % 4)], initial=0.0,
                                         op0=ALU.mult, op1=ALU.add)

                ev_pv = [None] * 16
                ev_pv[0] = p.run("gp", wait=[e for e in (ev_wq, evda[0]) if e],
                                 build=lambda r: pv(r, 0))
                ev_pv[1] = p.run("gp", wait=[evda[0]], build=lambda r: pv(r, 1))
                evpro = p.run("ve", wait=[e for e in (ev_wq, evda[0]) if e],
                              build=lambda r: (vtt(r, 0), vtt(r, 1)))
                evslot = [None] * 16
                for s in range(NS):
                    if s == 4:
                        evda[2] = mk_mkg(ic, 2, evd2c, evslot[3])
                    elif s == 8:
                        evda[3] = mk_mkg(ic, 3, evd2c, evslot[7])
                        if ic + 1 < DIT:
                            prep_next = mk_prep(ic + 1)
                    elif s == 11 and ic + 1 < DIT:
                        ev_wq_next = mk_wq(ic + 1, prep_next[1], prep_next[0])
                    elif s == 12 and ic + 1 < DIT:
                        evda_next[0] = mk_mkg(ic + 1, 0, prep_next[1], evslot[11])
                    if s + 2 < NS:
                        pw = [evda[(s + 2) // 4]] + \
                             ([evslot[s - 2]] if s >= 2 else [evpro])
                        ev_pv[s + 2] = p.run("gp", wait=[e for e in pw if e],
                                             build=lambda r, s2=s + 2: pv(r, s2))
                    def slot(r, s=s):
                        if s + 2 < NS:
                            vtt(r, s + 2)
                        scn(r, s)
                    wv = [lastGpHb[s % 3], lastCHb[s % 3], ev_pv[s]]
                    evslot[s] = p.run("ve", wait=[e for e in wv if e], build=slot)
                    lastCHb[s % 3] = p.run("sc", wait=[evslot[s]],
                        build=lambda r, s=s: r.activation(
                            out=hcol(s), in_=A6f[:, 2048 + (s % 3) * 512 + 511:
                                                    2048 + (s % 3) * 512 + 512],
                            func=AF.Copy, bias=0.0, scale=1.0))
                    pbuf = pb0 if s % 2 == 0 else pb1
                    evP = p.run("gp", wait=[e for e in (evslot[s], ev_accd[s % 2]) if e],
                        build=lambda r, s=s, pbuf=pbuf: r.tensor_tensor(
                            out=pbuf, in0=A6f[:, hbv(s % 3)], in1=Ccol(s), op=ALU.mult))
                    lastGpHb[s % 3] = evP
                    ev_gp_lastprod = evP
                    aw = [evP, ev_acc_prev] + ([ev_yz_prev] if s == 0 else [])
                    if s == 0:
                        ev_acc, ev_accd[0] = p.run("gp", wait=[e for e in aw if e],
                            build=lambda r: r.dma_start(out=yv, in_=pb0), defer=True)
                    else:
                        ev_acc, ev_accd[s % 2] = p.run("gp", wait=[e for e in aw if e],
                            build=lambda r, pbuf=pbuf: r.dma_start(
                                out=yv, in_=pbuf, accum_op=ALU.add), defer=True)
                    ev_acc_prev = ev_accd[s % 2]
                evyz = p.run("ve", wait=[e for e in (ev_accd[1], evsz, ev_yw_prev) if e],
                    build=lambda r: r.tensor_tensor(
                        out=A4b[:, 0:512], in0=yv, in1=A4b[:, 1536:2048], op=ALU.mult))
                evyw = p.run("sy", wait=[evyz], build=lambda r, q=q, ic=ic: r.dma_start(
                    out=yzF_d[ts(ic, 128), ts(q, 512)], in_=A4b[:, 0:512]))
                ev_out.append(evyw)
                ev_yz_prev, ev_yw_prev = evyz, evyw
                lastV15[par] = evslot[15]
                lastDl[par] = evda[3]
                ev_s11_prev, ev_s15_prev = evslot[11], evslot[15]
                if ic + 1 < DIT:
                    evda_next[1] = mk_mkg(ic + 1, 1, prep_next[1], evslot[15])
                    evu, evd2c = prep_next
                    ev_wq = ev_wq_next
                    ev_wq_prev = ev_wq
                    evda = evda_next
                    evda_next = [None] * 4
        evow = p.run("gp", wait=[ev_out[-1], ev_gp_lastprod], build=lambda r: r.dma_start(
            out=A1b[:, :].rearrange("p (ic o) -> p ic o", ic=DIT),
            in_=out_projW.rearrange("(ic p) o -> p ic o", p=128)))
        ev_fin = []
        evld2 = {}
        evmm2 = {}

        def yzload(j):
            tt3, ic3 = divmod(j, DIT)
            war = evmm2.get(j - 2)
            w0 = [e for e in (war, ev_out[-1]) if e]
            evld2[j] = p.run("sy", wait=w0,
                build=lambda r, tt3=tt3, ic3=ic3, j=j: r.dma_start(
                    out=A4b[:, (j % 2) * 128: (j % 2) * 128 + 128],
                    in_=yzF_d[ts(ic3, 128), ts(tt3, 128)]))

        yzload(0)
        yzload(1)
        for tt2 in range(TT):
            pb2 = 0 if tt2 % 2 == 0 else 2
            for ic in range(DIT):
                j = tt2 * DIT + ic
                def mmo2(r, ic=ic, j=j, pb2=pb2):
                    for ec2 in range(2):
                        r.matmul(psb[pb2 + ec2][:, 0:512],
                                 A4b[:, (j % 2) * 128: (j % 2) * 128 + 128],
                                 A1b[:, ic * 1024 + ec2 * 512: ic * 1024 + (ec2 + 1) * 512],
                                 start=(ic == 0), stop=(ic == DIT - 1))
                mw2 = [evld2[j]] + ([evow] if j == 0 else []) + \
                      ([ev_fin[tt2 - 2]] if (ic == 0 and tt2 >= 2) else [])
                evmm2[j] = p.run("te", wait=mw2, build=mmo2)
                if j + 2 < TT * DIT:
                    yzload(j + 2)
            def accx(r, tt2=tt2, pb2=pb2):
                for ec2 in range(2):
                    sl = xres[:, tt2 * D + ec2 * 512: tt2 * D + (ec2 + 1) * 512]
                    r.tensor_tensor(out=sl, in0=sl, in1=psb[pb2 + ec2][:, 0:512], op=ALU.add)
            ev_fin.append(p.run("ve", wait=[evmm2[tt2 * DIT + DIT - 1]], build=accx))
        return p.run("gp", wait=[ev_fin[-1], evwl[DIT - 1], evzl[DIT - 1]],
                     build=lambda r: r.nop())

    ev = attn_a(ev_ld)
    if upto >= 2:
        ev = ffn(ev, "B")
    if upto >= 3:
        ev = attn_b(ev)
    if upto >= 4:
        ev = ffn(ev, "D")

    ev_st = p.run("sy", wait=[ev] + p.bar(), build=lambda r: r.dma_start(
        out=y_out.rearrange("(tt p) d -> p tt d", p=128),
        in_=xres[:, :].rearrange("p (tt d) -> p tt d", tt=TT)))
    p.run("gp", wait=[ev_st] + p.bar(), build=lambda r: r.nop())

    with nc.Block() as block:
        p.emit(block, measure=(_counts is None))
    if _counts is None:
        return build(upto, _counts=p.measured)
    return nc, dd


def make_inputs(inp, core):
    f32 = lambda a: np.ascontiguousarray(np.asarray(a, dtype=np.float32))
    x = f32(inp["L0_attn_a_x"][core])
    m = {
        "x": x,
        "wqT": f32(np.asarray(inp["L0_attn_a_wq"]).T),
        "wkT": f32(np.asarray(inp["L0_attn_a_wk"]).T),
        "wvT": f32(np.asarray(inp["L0_attn_a_wv"]).T),
        "woT": f32(np.asarray(inp["L0_attn_a_wo"]).T),
        "rfTh": f32(np.vstack([np.asarray(inp["L0_attn_a_rf"]).T] * 2)),
        "in_projW": f32(inp["L0_attn_b_in_proj_w"]),
        "conv_wi": f32(inp["L0_attn_b_conv_w"]),
        "conv_bi": f32(np.asarray(inp["L0_attn_b_conv_b"]).reshape(DI, 1)),
        "x_projW": (lambda w: np.concatenate([
            w[:, 0:16], np.zeros((DI, 16), np.float32), w[:, 16:32],
            np.zeros((DI, 16), np.float32), w[:, 32:33], np.zeros((DI, 15), np.float32)],
            axis=1))(f32(inp["L0_attn_b_x_proj_w"])),
        "A_logi": f32(inp["L0_attn_b_A_log"]),
        "dt_wi": f32(inp["L0_attn_b_dt_w"]),
        "dt_bi": f32(np.asarray(inp["L0_attn_b_dt_b"]).reshape(DI, 1)),
        "out_projW": f32(inp["L0_attn_b_out_proj_w"]),
        "ln_gi": f32(np.asarray(inp["L0_attn_b_ln_g"]).reshape(1, D)),
        "ln_bi": f32(np.asarray(inp["L0_attn_b_ln_b"]).reshape(1, D)),
        "WgW": f32(inp["L0_ffn_Wg"]),
        "bgi": f32(np.asarray(inp["L0_ffn_bg"]).reshape(NE, 1)),
        "w1i": f32(inp["L0_ffn_w1"]),
        "w2i": f32(inp["L0_ffn_w2"]),
        "w3i": f32(inp["L0_ffn_w3"]),
        "bei": f32(np.concatenate([np.asarray(inp["L0_ffn_b1"]).ravel(),
                                   np.asarray(inp["L0_ffn_b2"]).ravel(),
                                   np.asarray(inp["L0_ffn_b3"]).ravel()]).reshape(1, 3 * NE)),
        "onesr": np.ones((1, 128), np.float32),
        "kkrep": np.broadcast_to(np.arange(1, 257, dtype=np.float32), (128, 256)).copy(),
        "bm2": np.repeat(np.eye(2, dtype=np.float32), 64, axis=0).reshape(128, 2),
        "selbi": np.kron(np.eye(16, dtype=np.float32), np.ones((1, 128), np.float32)),
        "idfi": np.eye(128, dtype=np.float32),
    }
    return m




_BUILT = None


def kernel(**inputs):
    global _BUILT
    if _BUILT is None:
        _BUILT = build(4)
    nc, dd = _BUILT
    in_maps = [make_inputs(inputs, c) for c in range(8)]
    res = run_bass_kernel_spmd(nc, in_maps, core_ids=list(range(8)))
    out = np.stack([np.asarray(r["y"], dtype=np.float32) for r in res.results], axis=0)
    return out
